# version 01
# baseline (speedup 1.0000x reference)
import math
import numpy as np
import ml_dtypes
import concourse.bass as bass
import concourse.mybir as mybir
from concourse.bass_utils import run_bass_kernel_spmd

F32 = mybir.dt.float32
BF16 = mybir.dt.bfloat16
AF = mybir.ActivationFunctionType
ALU = mybir.AluOpType
AX = mybir.AxisListType
bf16 = ml_dtypes.bfloat16

D = 4096
SEQ = 8192
NCORES = 8
TPC = 1024
NT = 8
NDC = D // 128
HD = 128
NH = 32
SCALE = 1.0 / math.sqrt(HD)
ROPE_THETA = 10000.0
LN_EPS = 1e-5
NORM_SAFETY = 1.03

L0 = dict(name="L0", kv=8, rope=True, qkv_bias=False, ffn="moe")
L1 = dict(name="L1", kv=32, rope=False, qkv_bias=True, ffn="chan")

SAME_ENGINE_EDGES = True



def shard_rows(core):
    idx = []
    for t in range(NT):
        a = 8 * t + core
        idx.extend(range(a * 128, (a + 1) * 128))
    return np.asarray(idx)


def fm_shard(x2d, core):
    return np.ascontiguousarray(x2d[shard_rows(core), :].T)


def unshard_fm(shards):
    out = np.empty((SEQ, D), dtype=np.asarray(shards[0]).dtype)
    for c in range(NCORES):
        out[shard_rows(c), :] = np.asarray(shards[c]).T
    return out


def wlay(WT, o_tile=128):
    Dd, O = WT.shape
    assert Dd == D and O % o_tile == 0
    a = WT.reshape(NDC, 128, O // o_tile, o_tile).transpose(1, 2, 0, 3)
    return np.ascontiguousarray(a.reshape(128, (O // o_tile) * NDC * o_tile))


def rope_tables(core):
    pos = shard_rows(core).astype(np.float64)
    inv = 1.0 / ROPE_THETA ** (np.arange(0, HD, 2, dtype=np.float64) / HD)
    fr = np.outer(inv, pos)
    cos = np.cos(fr).astype(np.float32)
    sin = np.sin(fr).astype(np.float32)
    return (np.ascontiguousarray(np.concatenate([cos, cos], axis=0)),
            np.ascontiguousarray(np.concatenate([-sin, sin], axis=0)))


def frontier_mask(core):
    m = np.zeros((128, 1024), dtype=np.float32)
    for j in range(8):
        if j < core:
            m[:, j * 128:(j + 1) * 128] = 1.0
        elif j == core:
            m[:, j * 128:(j + 1) * 128] = np.tril(np.ones((128, 128), np.float32))
    return m.astype(bf16)


CONSTS = dict(
    ones_col=np.ones((128, 1), np.float32),
    epscol=np.full((128, 1), LN_EPS, np.float32),
    ones_row=np.ones((1, 128), np.float32),
    negsc_row=np.full((1, 128), -SCALE * NORM_SAFETY, np.float32),
    identf=np.eye(128, dtype=np.float32),
    identb=np.eye(128, dtype=np.float32).astype(bf16),
    swapf=np.roll(np.eye(128, dtype=np.float32), 64, axis=1),
    expander=(2.0 * np.repeat(np.eye(8, dtype=np.float32), 16, axis=1)).astype(bf16),
)



ENGINES = ("sync", "gpsimd", "tensor", "vector", "scalar")


class Chain:
    __slots__ = ("h", "n", "name", "eng")

    def __init__(self, h, name, eng):
        self.h = h
        self.n = 0
        self.name = name
        self.eng = eng


class Buf:
    __slots__ = ("ap", "wr", "rd", "name")

    def __init__(self, ap, name=""):
        self.ap = ap
        self.wr = None
        self.rd = []
        self.name = name

    def __getitem__(self, sl):
        return self.ap[sl]

    @property
    def t(self):
        return self.ap


class P:
    def __init__(self, nc):
        self.nc = nc
        self.ops = []
        self._cm = []
        self._memo = {}
        self.ch = {e: self.chain("ch_" + e, e) for e in ENGINES}
        self.nops = 0

    def chain(self, name, eng):
        cm = self.nc.semaphore(name)
        h = cm.__enter__()
        self._cm.append(cm)
        return Chain(h, name, eng)

    def _wait_list(self, eng, deps, kind):
        w = []
        for d in deps:
            if d is None:
                continue
            ch, v, pkind = d
            if v <= 0:
                continue
            if (not SAME_ENGINE_EDGES and ch.eng == eng
                    and pkind == "compute" and kind == "compute"):
                continue
            key = (eng, ch.name)
            if self._memo.get(key, -1) >= v:
                continue
            self._memo[key] = v
            w.append((ch.h, v))
        return w

    def rec(self, eng, emit, reads=(), writes=(), kind="compute", extra=(),
            inc_n=None):
        ch = self.ch[eng]
        deps = []
        for b in reads:
            deps.append(b.wr)
        for b in writes:
            deps.append(b.wr)
            deps.extend(b.rd)
        deps.extend(extra)
        w = self._wait_list(eng, deps, kind)
        n = inc_n if inc_n is not None else (16 if kind == "dma" else 1)
        ch.n += n
        tok = (ch, ch.n, kind)

        def fn(e, _w=w, _emit=emit, _h=ch.h, _n=n):
            for hh, v in _w:
                e.wait_ge(hh, v)
            inst = _emit(e)
            inst.then_inc(_h, _n)
        self.ops.append((eng, fn))
        self.nops += 1
        for b in reads:
            b.rd.append(tok)
        for b in writes:
            b.wr = tok
            b.rd = []
        return tok

    def dma(self, queue, dst, src, reads=(), writes=(), extra=()):
        return self.rec(queue, lambda e: e.dma_start(out=dst, in_=src),
                        reads=reads, writes=writes, kind="dma", extra=extra)

    def barrier_all(self):
        toks = [(c, c.n, "dma") for c in self.ch.values() if c.n > 0]
        for eng in ENGINES:
            w = self._wait_list(eng, toks, "compute")
            if not w:
                continue

            def fn(e, _w=w):
                for hh, v in _w:
                    e.wait_ge(hh, v)
            self.ops.append((eng, fn))

    def finish(self):
        with self.nc.Block() as block:
            for name in ENGINES:
                sel = [fn for (eng, fn) in self.ops if eng == name]
                if not sel:
                    continue

                def body(e, _sel=sel):
                    for fn in _sel:
                        fn(e)
                getattr(block, name)(body)
        for cm in self._cm:
            cm.__exit__(None, None, None)


class Alloc:

    def __init__(self, nc):
        self.nc = nc
        self._cm = []

    def sb(self, name, shape, dt):
        cm = self.nc.sbuf_tensor(name, shape, dt)
        t = cm.__enter__()
        self._cm.append(cm)
        return Buf(t.ap() if hasattr(t, "ap") else t, name)

    def psum(self, name, shape, dt):
        cm = self.nc.psum_tensor(name, shape, dt)
        t = cm.__enter__()
        self._cm.append(cm)
        return Buf(t.ap() if hasattr(t, "ap") else t, name)

    def close(self):
        for cm in reversed(self._cm):
            cm.__exit__(None, None, None)



def make_pool(al):
    return {
        "PM": [al.psum(f"pool_pm{i}", [128, 512], F32) for i in range(4)],
        "PT": [al.psum(f"pool_pt{i}", [128, 512], BF16) for i in range(2)],
        "PO": [al.psum(f"pool_po{i}", [128, 128], F32) for i in range(2)],
    }


def emit_consts(p, al, io):
    c = {}
    for nm, shape, dt in (("ones_col", (128, 1), F32), ("epscol", (128, 1), F32),
                          ("ones_row", (1, 128), F32),
                          ("negsc_row", (1, 128), F32), ("identf", (128, 128), F32),
                          ("identb", (128, 128), BF16), ("swapf", (128, 128), F32),
                          ("expander", (8, 128), BF16),
                          ("maskM", (128, 1024), BF16)):
        b = al.sb("s_" + nm, list(shape), dt)
        p.dma("sync", b.ap[:, :], io[nm], writes=[b])
        c[nm] = b
    return c


def emit_layernorm(p, al, C, PS, SC, io_x, xn, xn_store=None, x_dep=None):
    xa = SC["f"][0:2]
    sq = SC["f"][2:4]
    pm = PS["PM"]

    for c in range(NDC):
        b = c % 2
        p.dma("sync", xa[b][:, :], io_x[c * 128:(c + 1) * 128, :],
              writes=[xa[b]], reads=[x_dep] if x_dep is not None else ())
        p.rec("scalar", lambda e, b=b: e.activation(sq[b][:, :], xa[b][:, :],
                                                    AF.Square),
              reads=[xa[b]], writes=[sq[b]])
        first, last = c == 0, c == NDC - 1
        for tb in range(2):
            sl = slice(tb * 512, (tb + 1) * 512)
            p.rec("tensor", lambda e, b=b, sl=sl, tb=tb, first=first, last=last:
                  e.matmul(pm[tb][0:1, :], C["ones_col"][:, :], xa[b][:, sl],
                           start=first, stop=last),
                  reads=[xa[b], C["ones_col"]],
                  writes=[] if not last else [pm[tb]],
                  extra=[pm[tb].wr] if first else ())
            p.rec("tensor", lambda e, b=b, sl=sl, tb=tb, first=first, last=last:
                  e.matmul(pm[2 + tb][0:1, :], C["ones_col"][:, :], sq[b][:, sl],
                           start=first, stop=last),
                  reads=[sq[b], C["ones_col"]],
                  writes=[] if not last else [pm[2 + tb]],
                  extra=[pm[2 + tb].wr] if first else ())

    st = SC["stat"]
    ns = SC["nslab"]

    def stv(i):
        if i == 2:
            return ns[0:1, :]
        return st[0:1, i * TPC:(i + 1) * TPC]

    def stv_h(i, tb):
        if i == 2:
            return ns[0:1, tb * 512:tb * 512 + 512]
        return st[0:1, i * TPC + tb * 512: i * TPC + tb * 512 + 512]

    for tb in range(2):
        p.rec("scalar", lambda e, tb=tb: e.activation(
            stv_h(0, tb), pm[tb][0:1, :], AF.Copy, scale=1.0 / D),
            reads=[pm[tb]], writes=[st])
        p.rec("scalar", lambda e, tb=tb: e.activation(
            stv_h(1, tb), pm[2 + tb][0:1, :], AF.Copy, scale=1.0 / D),
            reads=[pm[2 + tb]], writes=[st])
    p.rec("scalar", lambda e: e.activation(stv(2), stv(0), AF.Square),
          reads=[st], writes=[ns])
    p.rec("vector", lambda e: e.tensor_sub(stv(1), stv(1), stv(2)),
          reads=[st, ns], writes=[st])
    p.rec("scalar", lambda e: e.activation(stv(1), stv(1), AF.Sqrt,
                                           bias=C["epscol"][0:1, :]),
          reads=[st, C["epscol"]], writes=[st])
    p.rec("vector", lambda e: e.reciprocal(stv(1), stv(1)),
          reads=[st], writes=[st])
    p.rec("vector", lambda e: e.tensor_mul(stv(2), stv(0), stv(1)),
          reads=[st], writes=[ns])
    p.rec("vector", lambda e: e.tensor_scalar_mul(stv(2), stv(2), -1.0),
          reads=[ns], writes=[ns])

    abc = al.sb("ln_abc", [128, TPC], F32)
    bbc = al.sb("ln_bbc", [128, TPC], F32)
    tmp = SC["f"][2]
    for tb in range(2):
        sl = slice(tb * 512, (tb + 1) * 512)
        p.rec("tensor", lambda e, sl=sl, tb=tb: e.matmul(
            pm[tb][:, :], C["ones_row"][:, :], stv_h(1, tb),
            start=True, stop=True),
            reads=[st, C["ones_row"]], writes=[pm[tb]])
        p.rec("vector", lambda e, sl=sl, tb=tb: e.tensor_copy(abc[:, sl],
                                                              pm[tb][:, :]),
              reads=[pm[tb]], writes=[abc])
        p.rec("tensor", lambda e, sl=sl, tb=tb: e.matmul(
            pm[2 + tb][:, :], C["ones_row"][:, :], stv_h(2, tb),
            start=True, stop=True),
            reads=[ns, C["ones_row"]], writes=[pm[2 + tb]])
        p.rec("vector", lambda e, sl=sl, tb=tb: e.tensor_copy(bbc[:, sl],
                                                              pm[2 + tb][:, :]),
              reads=[pm[2 + tb]], writes=[bbc])

    for c in range(NDC):
        b = c % 2
        p.dma("sync", xa[b][:, :], io_x[c * 128:(c + 1) * 128, :],
              writes=[xa[b]], reads=[x_dep] if x_dep is not None else ())
        p.rec("vector", lambda e, b=b: e.tensor_mul(tmp[:, :], xa[b][:, :],
                                                    abc[:, :]),
              reads=[xa[b], abc], writes=[tmp])
        p.rec("vector", lambda e, c=c: e.tensor_add(
            xn[:, c * TPC:(c + 1) * TPC], tmp[:, :], bbc[:, :]),
            reads=[tmp, bbc], writes=[xn])
        if xn_store is not None:
            p.dma("scalar", xn_store[c * 128:(c + 1) * 128, :],
                  xn[:, c * TPC:(c + 1) * TPC], reads=[xn])
    return


def emit_projections(p, al, C, PS, SC, cfg, io, xn):
    KV = cfg["kv"]
    wbuf = SC["w"]
    pm = PS["PM"][0:2]
    pn = PS["PM"][2:4]
    pt = PS["PT"]
    ktile = SC["f"][0]
    rtmp = SC["f"][1]
    rtile = SC["b"][0]
    vtr = SC["b"][1]
    nsq = SC["f"][2]
    nslab = SC["nslab"]
    if cfg["rope"]:
        cosT = al.sb("pj_cos", [128, TPC], F32)
        sinS = al.sb("pj_sin", [128, TPC], F32)
        p.dma("sync", cosT[:, :], io["cosT"], writes=[cosT])
        p.dma("sync", sinS[:, :], io["sinS"], writes=[sinS])
    if cfg["qkv_bias"]:
        qkvb = al.sb("pj_qkvb", [128, 3 * NDC], F32)
        p.dma("sync", qkvb[:, :], io["qkvb"], writes=[qkvb])

    state = {"w": 0}

    def head(kind, h, wsrc, bias_col, do_rope, norm_row):
        b = state["w"] % 2
        state["w"] += 1
        p.dma("gpsimd", wbuf[b][:, :], wsrc, writes=[wbuf[b]])
        for tb in range(2):
            for c in range(NDC):
                p.rec("tensor", lambda e, b=b, c=c, tb=tb: e.matmul(
                    pm[tb][:, :], wbuf[b][:, c * 128:(c + 1) * 128],
                    xn[:, c * TPC + tb * 512: c * TPC + tb * 512 + 512],
                    start=(c == 0), stop=(c == NDC - 1)),
                    reads=[wbuf[b], xn],
                    writes=[pm[tb]] if c == NDC - 1 else [],
                    extra=([pm[tb].wr] + pm[tb].rd) if c == 0 else ())
        for tb in range(2):
            sl = slice(tb * 512, (tb + 1) * 512)
            if bias_col is None:
                p.rec("scalar", lambda e, tb=tb, sl=sl: e.activation(
                    ktile[:, sl], pm[tb][:, :], AF.Copy),
                    reads=[pm[tb]], writes=[ktile] if tb == 1 else [],
                    extra=(ktile.rd + [ktile.wr]) if tb == 0 else ())
            else:
                p.rec("vector", lambda e, tb=tb, sl=sl, bc=bias_col:
                      e.tensor_scalar_add(ktile[:, sl], pm[tb][:, :],
                                          qkvb[:, bc:bc + 1]),
                      reads=[pm[tb], qkvb], writes=[ktile] if tb == 1 else [],
                      extra=(ktile.rd + [ktile.wr]) if tb == 0 else ())
        if norm_row is not None:
            p.rec("scalar", lambda e: e.activation(nsq[:, :], ktile[:, :],
                                                   AF.Square),
                  reads=[ktile], writes=[nsq])
            for tb in range(2):
                sl = slice(tb * 512, (tb + 1) * 512)
                p.rec("tensor", lambda e, tb=tb, sl=sl: e.matmul(
                    pn[tb][0:1, :], C["ones_col"][:, :], nsq[:, sl],
                    start=True, stop=True),
                    reads=[nsq, C["ones_col"]], writes=[pn[tb]])
                p.rec("vector", lambda e, tb=tb, sl=sl: e.tensor_copy(
                    nslab[0:1, sl], pn[tb][0:1, :]),
                    reads=[pn[tb]], writes=[nslab] if tb == 1 else [],
                    extra=(nslab.rd + [nslab.wr]) if tb == 0 else ())
            p.dma("scalar", norm_row, nslab[0:1, :], reads=[nslab])
        if do_rope:
            for tb in range(2):
                sl = slice(tb * 512, (tb + 1) * 512)
                p.rec("tensor", lambda e, tb=tb, sl=sl: e.matmul(
                    pn[tb][:, :], C["swapf"][:, :], ktile[:, sl],
                    start=True, stop=True),
                    reads=[ktile, C["swapf"]], writes=[pn[tb]])
                p.rec("vector", lambda e, tb=tb, sl=sl: e.tensor_mul(
                    rtmp[:, sl], pn[tb][:, :], sinS[:, sl]),
                    reads=[pn[tb], sinS], writes=[rtmp] if tb == 1 else [],
                    extra=(rtmp.rd + [rtmp.wr]) if tb == 0 else ())
            p.rec("vector", lambda e: e.tensor_mul(ktile[:, :], ktile[:, :],
                                                   cosT[:, :]),
                 reads=[ktile, cosT], writes=[ktile])
            p.rec("vector", lambda e: e.tensor_add(rtile[:, :], ktile[:, :],
                                                   rtmp[:, :]),
                 reads=[ktile, rtmp], writes=[rtile])
        else:
            p.rec("vector", lambda e: e.tensor_copy(rtile[:, :], ktile[:, :]),
                  reads=[ktile], writes=[rtile])
        return rtile

    for h in range(KV):
        head("k", h, io["WkT"][:, h * (NDC * 128):(h + 1) * (NDC * 128)],
             (NDC + h) if cfg["qkv_bias"] else None, cfg["rope"],
             io["knorm_out"][h:h + 1, :])
        p.dma("scalar", io["kT_out"][:, h * TPC:(h + 1) * TPC], rtile[:, :],
              reads=[rtile])

    for h in range(KV):
        head("v", h, io["WvT"][:, h * (NDC * 128):(h + 1) * (NDC * 128)],
             (2 * NDC + h) if cfg["qkv_bias"] else None, False, None)
        for grp in range(2):
            for j in range(4):
                tt = grp * 4 + j
                p.rec("tensor", lambda e, tt=tt, j=j, grp=grp: e.transpose(
                    pt[grp][:, j * 128:(j + 1) * 128],
                    rtile[:, tt * 128:(tt + 1) * 128], C["identb"][:, :]),
                    reads=[rtile, C["identb"]],
                    writes=[pt[grp]] if j == 3 else [],
                    extra=([pt[grp].wr] + pt[grp].rd) if j == 0 else ())
            p.rec("vector", lambda e, grp=grp: e.tensor_copy(
                vtr[:, grp * 512:(grp + 1) * 512], pt[grp][:, :]),
                reads=[pt[grp]], writes=[vtr] if grp == 1 else [],
                extra=(vtr.rd + [vtr.wr]) if grp == 0 else ())
        for tt in range(NT):
            p.dma("scalar",
                  io["v_out"][tt * 128:(tt + 1) * 128, h * 128:(h + 1) * 128],
                  vtr[:, tt * 128:(tt + 1) * 128], reads=[vtr])

    for h in range(NH):
        head("q", h, io["WqT"][:, h * (NDC * 128):(h + 1) * (NDC * 128)],
             h if cfg["qkv_bias"] else None, cfg["rope"],
             io["qnorm_out"][h:h + 1, :])
        p.dma("scalar", io["qT_out"][:, h * TPC:(h + 1) * TPC], rtile[:, :],
              reads=[rtile])


def emit_router(p, al, C, PS, SC, io_wT, xn, topk, tag="", dbg=None):
    wr = al.sb("rt_w" + tag, [128, NDC * 8], BF16)
    p.dma("gpsimd", wr[:, :], io_wT, writes=[wr])
    pm = PS["PM"][0:2]
    ptt = PS["PM"][2]
    sm = SC["small"]
    LT = slice(0, 8)
    PT_ = slice(8, 16)
    M1, NM1, SU, RS = 16, 17, 18, 19
    M2 = 20
    MS = slice(24, 32)
    lgT = SC["f"][3]
    cmbT = SC["cmbT"]
    for tb in range(2):
        sl = slice(tb * 512, (tb + 1) * 512)
        for c in range(NDC):
            p.rec("tensor", lambda e, c=c, tb=tb: e.matmul(
                pm[tb][0:8, :], wr[:, c * 8:(c + 1) * 8],
                xn[:, c * TPC + tb * 512: c * TPC + tb * 512 + 512],
                start=(c == 0), stop=(c == NDC - 1)),
                reads=[wr, xn], writes=[pm[tb]] if c == NDC - 1 else [],
                extra=([pm[tb].wr] + pm[tb].rd) if c == 0 else ())
        p.rec("vector", lambda e, tb=tb, sl=sl: e.tensor_copy(lgT[0:8, sl],
                                                              pm[tb][0:8, :]),
              reads=[pm[tb]], writes=[lgT] if tb == 1 else [],
              extra=(lgT.rd + [lgT.wr]) if tb == 0 else ())
    if dbg is not None:
        p.dma("scalar", dbg["dbg_lgT"], lgT[0:8, :], reads=[lgT])
    for tt in range(NT):
        p.rec("tensor", lambda e, tt=tt: e.transpose(
            ptt[:, 0:8], lgT[0:8, tt * 128:(tt + 1) * 128],
            C["identf"][0:8, 0:8]),
            reads=[lgT, C["identf"]], writes=[ptt])
        p.rec("vector", lambda e: e.tensor_copy(sm[:, LT], ptt[:, 0:8]),
              reads=[ptt], writes=[sm])
        if dbg is not None and tt == 0:
            p.dma("scalar", dbg["dbg_ltok"], sm[:, 0:32], reads=[sm])
        p.rec("vector", lambda e: e.reduce_max(sm[:, M1:NM1], sm[:, LT], axis=AX.X),
              reads=[sm], writes=[sm])
        p.rec("vector", lambda e: e.tensor_scalar_mul(sm[:, NM1:SU], sm[:, M1:NM1],
                                                      -1.0),
              reads=[sm], writes=[sm])
        p.rec("scalar", lambda e: e.activation(sm[:, PT_], sm[:, LT], AF.Exp,
                                               bias=sm[:, NM1:SU],
                                               accum_out=sm[:, SU:RS]),
              reads=[sm], writes=[sm])
        p.rec("vector", lambda e: e.reciprocal(sm[:, RS:RS + 1], sm[:, SU:RS]),
              reads=[sm], writes=[sm])
        p.rec("vector", lambda e: e.tensor_scalar_mul(sm[:, PT_], sm[:, PT_],
                                                      sm[:, RS:RS + 1]),
              reads=[sm], writes=[sm])
        if topk:
            p.rec("vector", lambda e: e.reduce_max(sm[:, M1:NM1], sm[:, PT_],
                                                   axis=AX.X),
                  reads=[sm], writes=[sm])
            p.rec("vector", lambda e: e.tensor_single_scalar(
                sm[:, MS], sm[:, PT_], sm[:, M1:NM1], ALU.is_lt),
                reads=[sm], writes=[sm])
            p.rec("vector", lambda e: e.tensor_mul(sm[:, MS], sm[:, PT_],
                                                   sm[:, MS]),
                  reads=[sm], writes=[sm])
            p.rec("vector", lambda e: e.reduce_max(sm[:, M2:M2 + 1], sm[:, MS],
                                                   axis=AX.X),
                  reads=[sm], writes=[sm])
            p.rec("vector", lambda e: e.tensor_single_scalar(
                sm[:, MS], sm[:, PT_], sm[:, M2:M2 + 1], ALU.is_ge),
                reads=[sm], writes=[sm])
            p.rec("vector", lambda e: e.tensor_mul(sm[:, PT_], sm[:, PT_],
                                                   sm[:, MS]),
                  reads=[sm], writes=[sm])
            p.rec("vector", lambda e: e.tensor_add(sm[:, SU:RS], sm[:, M1:NM1],
                                                   sm[:, M2:M2 + 1]),
                  reads=[sm], writes=[sm])
            p.rec("vector", lambda e: e.reciprocal(sm[:, RS:RS + 1], sm[:, SU:RS]),
                  reads=[sm], writes=[sm])
            p.rec("vector", lambda e: e.tensor_scalar_mul(sm[:, PT_], sm[:, PT_],
                                                          sm[:, RS:RS + 1]),
                  reads=[sm], writes=[sm])
        if dbg is not None and tt == 0:
            p.dma("scalar", dbg["dbg_ptok"], sm[:, 0:32], reads=[sm])
        p.rec("tensor", lambda e: e.transpose(ptt[0:8, 0:128], sm[:, PT_],
                                              C["identf"][:, :]),
              reads=[sm, C["identf"]], writes=[ptt])
        p.rec("vector", lambda e, tt=tt: e.tensor_copy(
            cmbT[0:8, tt * 128:(tt + 1) * 128], ptt[0:8, 0:128]),
            reads=[ptt], writes=[cmbT] if tt == NT - 1 else [],
            extra=(cmbT.rd + [cmbT.wr]) if tt == 0 else ())
    return cmbT


def emit_ffn_moe(p, al, C, PS, SC, io, xn):
    cmbT = emit_router(p, al, C, PS, SC, io["WrT"], xn, topk=True,
                       dbg=io if "dbg_lgT" in io else None)
    atb = al.sb("mo_at", [128, NDC * 128], BF16)
    btb = al.sb("mo_bt", [128, D], BF16)
    p.dma("gpsimd", atb[:, :], io["AT"], writes=[atb])
    p.dma("gpsimd", btb[:, :], io["BT"], writes=[btb])
    bb = al.sb("mo_bb", [128, NDC], F32)
    p.dma("sync", bb[:, :], io["bb"], writes=[bb])
    pm = PS["PM"][0:2]
    pc = PS["PM"][2:4]
    hT = SC["b"][0]
    cb = SC["f"][0]
    hw = SC["b"][1]
    for tb in range(2):
        sl = slice(tb * 512, (tb + 1) * 512)
        for c in range(NDC):
            p.rec("tensor", lambda e, c=c, tb=tb: e.matmul(
                pm[tb][:, :], atb[:, c * 128:(c + 1) * 128],
                xn[:, c * TPC + tb * 512: c * TPC + tb * 512 + 512],
                start=(c == 0), stop=(c == NDC - 1)),
                reads=[atb, xn], writes=[pm[tb]] if c == NDC - 1 else [],
                extra=([pm[tb].wr] + pm[tb].rd) if c == 0 else ())
        p.rec("scalar", lambda e, tb=tb, sl=sl: e.activation(
            hT[:, sl], pm[tb][:, :], AF.Copy),
            reads=[pm[tb]], writes=[hT] if tb == 1 else [],
            extra=(hT.rd + [hT.wr]) if tb == 0 else ())
        p.rec("tensor", lambda e, tb=tb: e.matmul(
            pc[tb][:, :], C["expander"][:, :], cmbT[:, tb * 512:(tb + 1) * 512],
            start=True, stop=True),
            reads=[C["expander"], cmbT], writes=[pc[tb]])
        p.rec("vector", lambda e, tb=tb, sl=sl: e.tensor_copy(cb[:, sl],
                                                              pc[tb][:, :]),
              reads=[pc[tb]], writes=[cb] if tb == 1 else [],
              extra=(cb.rd + [cb.wr]) if tb == 0 else ())
    p.rec("vector", lambda e: e.tensor_mul(hw[:, :], hT[:, :], cb[:, :]),
          reads=[hT, cb], writes=[hw])
    if "dbg_hT" in io:
        p.dma("scalar", io["dbg_hT"], hT[:, :], reads=[hT])
        p.dma("scalar", io["dbg_cb"], cb[:, :], reads=[cb])
        p.dma("scalar", io["dbg_hw"], hw[:, :], reads=[hw])
        p.dma("scalar", io["dbg_cmbT"], cmbT[0:8, :], reads=[cmbT])

    wbuf = SC["w"]
    ffb = SC["b"][0]
    for oc in range(NDC):
        b = oc % 2
        p.dma("gpsimd", wbuf[b][:, :],
              io["WbT"][:, oc * (NDC * 128):(oc + 1) * (NDC * 128)],
              writes=[wbuf[b]])
        for tb in range(2):
            for c in range(NDC):
                p.rec("tensor", lambda e, b=b, c=c, tb=tb: e.matmul(
                    pm[tb][:, :], wbuf[b][:, c * 128:(c + 1) * 128],
                    xn[:, c * TPC + tb * 512: c * TPC + tb * 512 + 512],
                    start=(c == 0), stop=False),
                    reads=[wbuf[b], xn], writes=[],
                    extra=([pm[tb].wr] + pm[tb].rd) if c == 0 else ())
            p.rec("tensor", lambda e, tb=tb, oc=oc: e.matmul(
                pm[tb][:, :], btb[:, oc * 128:(oc + 1) * 128],
                hw[:, tb * 512:(tb + 1) * 512], start=False, stop=True),
                reads=[btb, hw], writes=[pm[tb]])
        for tb in range(2):
            sl = slice(tb * 512, (tb + 1) * 512)
            p.rec("vector", lambda e, tb=tb, sl=sl, oc=oc: e.tensor_scalar_add(
                ffb[:, sl], pm[tb][:, :], bb[:, oc:oc + 1]),
                reads=[pm[tb], bb], writes=[ffb] if tb == 1 else [],
                extra=(ffb.rd + [ffb.wr]) if tb == 0 else ())
        p.dma("scalar", io["ffn_out"][oc * 128:(oc + 1) * 128, :], ffb[:, :],
              reads=[ffb])


def emit_ffn_chan(p, al, C, PS, SC, io, xn):
    cmbT = emit_router(p, al, C, PS, SC, io["gateT"], xn, topk=False)
    ics = al.sb("ch_ics", [8, D], BF16)
    ocs = al.sb("ch_ocs", [8, D], BF16)
    p.dma("gpsimd", ics[:, :], io["ics"], writes=[ics])
    p.dma("gpsimd", ocs[:, :], io["ocs"], writes=[ocs])
    pm = PS["PM"][0:2]
    po = PS["PM"][2:4]
    for c in range(NDC):
        for tb in range(2):
            sl = slice(c * TPC + tb * 512, c * TPC + tb * 512 + 512)
            p.rec("tensor", lambda e, c=c, tb=tb: e.matmul(
                pm[tb][:, :], ics[:, c * 128:(c + 1) * 128],
                cmbT[:, tb * 512:(tb + 1) * 512], start=True, stop=True),
                reads=[ics, cmbT], writes=[pm[tb]])
            p.rec("vector", lambda e, sl=sl, tb=tb: e.tensor_mul(
                xn[:, sl], xn[:, sl], pm[tb][:, :]),
                reads=[xn, pm[tb]], writes=[xn])
    wbuf = SC["w"]
    wsg = wbuf
    osc = SC["f"][0]
    ffo = SC["b"][0]
    for oc in range(NDC):
        b = oc % 2
        p.dma("gpsimd", wbuf[b][:, :],
              io["WsT"][:, oc * (NDC * 128):(oc + 1) * (NDC * 128)],
              writes=[wbuf[b]])
        p.rec("scalar", lambda e, b=b: e.sign(wsg[b][:, :], wbuf[b][:, :]),
              reads=[wbuf[b]], writes=[wbuf[b]])
        for tb in range(2):
            for c in range(NDC):
                p.rec("tensor", lambda e, b=b, c=c, tb=tb: e.matmul(
                    pm[tb][:, :], wsg[b][:, c * 128:(c + 1) * 128],
                    xn[:, c * TPC + tb * 512: c * TPC + tb * 512 + 512],
                    start=(c == 0), stop=(c == NDC - 1)),
                    reads=[wsg[b], xn], writes=[pm[tb]] if c == NDC - 1 else [],
                    extra=([pm[tb].wr] + pm[tb].rd) if c == 0 else ())
            p.rec("tensor", lambda e, tb=tb, oc=oc: e.matmul(
                po[tb][:, :], ocs[:, oc * 128:(oc + 1) * 128],
                cmbT[:, tb * 512:(tb + 1) * 512], start=True, stop=True),
                reads=[ocs, cmbT], writes=[po[tb]])
        for tb in range(2):
            sl = slice(tb * 512, (tb + 1) * 512)
            p.rec("vector", lambda e, tb=tb, sl=sl: e.tensor_copy(
                osc[:, sl], po[tb][:, :]),
                reads=[po[tb]], writes=[osc] if tb == 1 else [],
                extra=(osc.rd + [osc.wr]) if tb == 0 else ())
            p.rec("vector", lambda e, tb=tb, sl=sl: e.tensor_mul(
                ffo[:, sl], pm[tb][:, :], osc[:, sl]),
                reads=[pm[tb], osc], writes=[ffo] if tb == 1 else [],
                extra=(ffo.rd + [ffo.wr]) if tb == 0 else ())
        p.dma("scalar", io["ffn_out"][oc * 128:(oc + 1) * 128, :], ffo[:, :],
              reads=[ffo])


def make_scratch(al, tag=""):
    return {
        "f": [al.sb(f"sc_f{i}{tag}", [128, TPC], F32) for i in range(4)],
        "b": [al.sb(f"sc_b{i}{tag}", [128, TPC], BF16) for i in range(2)],
        "w": [al.sb(f"sc_w{i}{tag}", [128, NDC * 128], BF16) for i in range(2)],
        "small": al.sb(f"sc_small{tag}", [128, 64], F32),
        "stat": al.sb(f"sc_stat{tag}", [1, 2 * TPC], F32),
        "nslab": al.sb(f"sc_nslab{tag}", [1, TPC], F32),
        "cmbT": al.sb(f"sc_cmbT{tag}", [8, TPC], BF16),
    }


def emit_prologue(p, al, C, PS, SC, cfg, io, xn=None, x_dep=None):
    if xn is None:
        xn = al.sb("xn_" + cfg["name"], [128, NDC * TPC], BF16)
    emit_layernorm(p, al, C, PS, SC, io["x"], xn,
                   xn_store=io.get("xn_out"), x_dep=x_dep)
    emit_projections(p, al, C, PS, SC, cfg, io, xn)
    if cfg["ffn"] == "moe":
        emit_ffn_moe(p, al, C, PS, SC, io, xn)
    else:
        emit_ffn_chan(p, al, C, PS, SC, io, xn)
    return xn



def _dram(nc, name, shape, dt, kind):
    return nc.dram_tensor(name, list(shape), dt, kind=kind).ap()


def build_p1():
    nc = bass.Bass("TRN2", target_bir_lowering=False, debug=False)
    KV = L0["kv"]
    io = {
        "x": _dram(nc, "x", (D, TPC), F32, "ExternalInput"),
        "WqT": _dram(nc, "WqT", (128, NH * NDC * 128), F32, "ExternalInput"),
        "WkT": _dram(nc, "WkT", (128, KV * NDC * 128), F32, "ExternalInput"),
        "WvT": _dram(nc, "WvT", (128, KV * NDC * 128), F32, "ExternalInput"),
        "WrT": _dram(nc, "WrT", (128, NDC * 8), F32, "ExternalInput"),
        "AT": _dram(nc, "AT", (128, NDC * 128), F32, "ExternalInput"),
        "BT": _dram(nc, "BT", (128, D), F32, "ExternalInput"),
        "WbT": _dram(nc, "WbT", (128, NDC * NDC * 128), F32, "ExternalInput"),
        "bb": _dram(nc, "bb", (128, NDC), F32, "ExternalInput"),
        "cosT": _dram(nc, "cosT", (128, TPC), F32, "ExternalInput"),
        "sinS": _dram(nc, "sinS", (128, TPC), F32, "ExternalInput"),
        "xn_out": _dram(nc, "xn_out", (D, TPC), BF16, "ExternalOutput"),
        "kT_out": _dram(nc, "kT_out", (128, KV * TPC), BF16, "ExternalOutput"),
        "v_out": _dram(nc, "v_out", (TPC, KV * 128), BF16, "ExternalOutput"),
        "qT_out": _dram(nc, "qT_out", (128, NH * TPC), BF16, "ExternalOutput"),
        "knorm_out": _dram(nc, "knorm_out", (KV, TPC), F32, "ExternalOutput"),
        "qnorm_out": _dram(nc, "qnorm_out", (NH, TPC), F32, "ExternalOutput"),
        "ffn_out": _dram(nc, "ffn_out", (D, TPC), BF16, "ExternalOutput"),
    }
    for nm, arr in CONSTS.items():
        io[nm] = _dram(nc, "c_" + nm, arr.shape,
                       BF16 if arr.dtype == bf16 else F32, "ExternalInput")
    io["maskM"] = _dram(nc, "c_maskM", (128, 1024), BF16, "ExternalInput")
    io["dbg_hT"] = _dram(nc, "dbg_hT", (128, TPC), BF16, "ExternalOutput")
    io["dbg_cb"] = _dram(nc, "dbg_cb", (128, TPC), F32, "ExternalOutput")
    io["dbg_hw"] = _dram(nc, "dbg_hw", (128, TPC), BF16, "ExternalOutput")
    io["dbg_cmbT"] = _dram(nc, "dbg_cmbT", (8, TPC), BF16, "ExternalOutput")
    io["dbg_lgT"] = _dram(nc, "dbg_lgT", (8, TPC), F32, "ExternalOutput")
    io["dbg_ltok"] = _dram(nc, "dbg_ltok", (128, 32), F32, "ExternalOutput")
    io["dbg_ptok"] = _dram(nc, "dbg_ptok", (128, 32), F32, "ExternalOutput")

    p = P(nc)
    al = Alloc(nc)
    PS = make_pool(al)
    SC = make_scratch(al)
    C = emit_consts(p, al, io)
    emit_prologue(p, al, C, PS, SC, L0, io)
    p.barrier_all()
    p.finish()
    al.close()
    return nc


def p1_inputs(inputs, core):
    x2d = np.asarray(inputs["L0_attn_hidden_states"]).reshape(SEQ, D)
    cosT, sinS = rope_tables(core)
    d = {
        "x": fm_shard(x2d, core),
        "WqT": wlay(np.ascontiguousarray(np.asarray(inputs["L0_attn_Wq"]).T)),
        "WkT": wlay(np.ascontiguousarray(np.asarray(inputs["L0_attn_Wk"]).T)),
        "WvT": wlay(np.ascontiguousarray(np.asarray(inputs["L0_attn_Wv"]).T)),
        "WrT": wlay(np.ascontiguousarray(np.asarray(inputs["L0_ffn_W_router"]).T),
                    o_tile=8),
        "AT": wlay(np.ascontiguousarray(
            np.asarray(inputs["L0_ffn_A"]).transpose(2, 0, 1).reshape(D, 128))),
        "BT": np.ascontiguousarray(
            np.asarray(inputs["L0_ffn_B"]).transpose(0, 2, 1).reshape(128, D)),
        "WbT": wlay(np.ascontiguousarray(np.asarray(inputs["L0_ffn_W_base"]).T)),
        "bb": np.ascontiguousarray(
            np.asarray(inputs["L0_ffn_b_base"]).reshape(NDC, 128).T),
        "cosT": cosT,
        "sinS": sinS,
        "c_maskM": frontier_mask(core),
    }
    for nm, arr in CONSTS.items():
        d["c_" + nm] = arr
    return d


def kernel(**inputs):
    raise NotImplementedError("P2/P3 not wired yet")



def emit_attention(p, al, C, PS, SC, cfg, io, att):
    KV = cfg["kv"]
    GS = NH // KV
    kbuf = al.sb("at_k", [128, SEQ], BF16)
    vbuf = al.sb("at_v", [128, SEQ], BF16)
    pstrip = al.sb("at_p", [128, SEQ], BF16)
    ptb = [al.sb(f"at_ptb{i}", [128, 512], BF16) for i in range(2)]
    qb = [al.sb(f"at_q{i}", [128, TPC], BF16) for i in range(2)]
    lsl = al.sb("at_l", [128, 8], F32)
    bias_all = al.sb("at_bias", [128, NH * NT], F32)
    PM = PS["PM"]
    PT = PS["PT"]
    PO = PS["PO"]

    knc = SC["f"][3]
    bkc = al.sb("at_bkc", [KV, 12], F32)
    bkrep = al.sb("at_bkrep", [128, KV], F32)
    bkrow = al.sb("at_bkrow", [1, KV], F32)
    qn = al.sb("at_qn", [NH, TPC], F32)
    qnt = al.sb("at_qnt", [128, NH], F32)
    for ci in range(8):
        p.dma("sync", knc[0:KV, 0:1024],
              io["knorm_all"][:, ci * 1024:(ci + 1) * 1024], writes=[knc])
        p.rec("vector", lambda e, ci=ci: e.reduce_max(
            bkc[:, ci:ci + 1], knc[0:KV, 0:1024], axis=AX.X),
            reads=[knc], writes=[bkc])
    p.rec("vector", lambda e: e.reduce_max(bkc[:, 8:9], bkc[:, 0:8], axis=AX.X),
          reads=[bkc], writes=[bkc])
    p.rec("scalar", lambda e: e.activation(bkc[:, 9:10], bkc[:, 8:9], AF.Sqrt),
          reads=[bkc], writes=[bkc])
    p.rec("tensor", lambda e: e.transpose(PM[0][0:1, 0:KV], bkc[:, 9:10],
                                          C["identf"][0:KV, 0:KV]),
          reads=[bkc, C["identf"]], writes=[PM[0]])
    p.rec("vector", lambda e: e.tensor_copy(bkrow[0:1, 0:KV], PM[0][0:1, 0:KV]),
          reads=[PM[0]], writes=[bkrow])
    p.rec("tensor", lambda e: e.matmul(PM[1][:, 0:KV], C["negsc_row"][:, :],
                                       bkrow[0:1, 0:KV], start=True, stop=True),
          reads=[bkrow, C["negsc_row"]], writes=[PM[1]])
    p.rec("vector", lambda e: e.tensor_copy(bkrep[:, :], PM[1][:, 0:KV]),
          reads=[PM[1]], writes=[bkrep])
    p.dma("sync", qn[:, :], io["qnorm"], writes=[qn])
    p.rec("scalar", lambda e: e.activation(qn[:, :], qn[:, :], AF.Sqrt),
          reads=[qn], writes=[qn])
    for t in range(NT):
        p.rec("tensor", lambda e, t=t: e.transpose(
            PM[0][:, 0:NH], qn[0:NH, t * 128:(t + 1) * 128],
            C["identf"][0:NH, 0:NH]),
            reads=[qn, C["identf"]], writes=[PM[0]])
        p.rec("vector", lambda e: e.tensor_copy(qnt[:, :], PM[0][:, 0:NH]),
              reads=[PM[0]], writes=[qnt])
        for h in range(NH):
            g = h // GS
            p.rec("vector", lambda e, h=h, t=t, g=g: e.tensor_single_scalar(
                bias_all[:, h * NT + t: h * NT + t + 1], qnt[:, h:h + 1],
                bkrep[:, g:g + 1], ALU.mult),
                reads=[qnt, bkrep], writes=[] if (h, t) != (NH - 1, NT - 1)
                else [bias_all],
                extra=(bias_all.rd + [bias_all.wr]) if (h, t) == (0, 0) else ())

    for g in range(KV):
        p.dma("sync", kbuf[:, :], io["kall"][:, g * SEQ:(g + 1) * SEQ],
              writes=[kbuf])
        p.dma("sync", vbuf[:, :],
              io["vall_r"][:, :, g * 128:(g + 1) * 128], writes=[vbuf])
        for hh in range(GS):
            h = g * GS + hh
            qp = h % 2
            p.dma("sync", qb[qp][:, :], io["qT"][:, h * TPC:(h + 1) * TPC],
                  writes=[qb[qp]])
            for t in range(NT):
                ext = (t + 1) * 1024
                nkc = ext // 128
                bias_col = bias_all[:, h * NT + t: h * NT + t + 1]
                np_pieces = [(off, min(512, ext - off))
                             for off in range(0, ext, 512)]
                nbulk = 0
                for pi, (off, ln) in enumerate(np_pieces):
                    pp = pi % 2
                    is_bulk = off < t * 1024
                    p.rec("tensor", lambda e, off=off, ln=ln, pp=pp, t=t, qp=qp:
                          e.matmul(PM[pp][:, 0:ln],
                                   qb[qp][:, t * 128:(t + 1) * 128],
                                   kbuf[:, off:off + ln],
                                   start=True, stop=True),
                          reads=[qb[qp], kbuf], writes=[PM[pp]])
                    p.rec("scalar", lambda e, off=off, ln=ln, pp=pp, bc=bias_col:
                          e.activation(pstrip[:, off:off + ln], PM[pp][:, 0:ln],
                                       AF.Exp, bias=bc, scale=SCALE,
                                       accum_out=lsl[:, 1 + pp:2 + pp]),
                          reads=[PM[pp], bias_all], writes=[lsl, pstrip])
                    if is_bulk:
                        if nbulk == 0:
                            p.rec("vector", lambda e, pp=pp: e.tensor_copy(
                                lsl[:, 0:1], lsl[:, 1 + pp:2 + pp]),
                                reads=[lsl], writes=[lsl])
                        else:
                            p.rec("vector", lambda e, pp=pp: e.tensor_add(
                                lsl[:, 0:1], lsl[:, 0:1], lsl[:, 1 + pp:2 + pp]),
                                reads=[lsl], writes=[lsl])
                        nbulk += 1
                if "dbg_ps" in io and h == 0 and t == 0:
                    p.dma("scalar", io["dbg_pre"], pstrip[:, 0:1024],
                          reads=[pstrip])
                    p.dma("scalar", io["dbg_msk"], C["maskM"][:, :],
                          reads=[C["maskM"]])
                p.rec("vector", lambda e, t=t: e.tensor_mul(
                    pstrip[:, t * 1024:(t + 1) * 1024],
                    pstrip[:, t * 1024:(t + 1) * 1024], C["maskM"][:, :]),
                    reads=[C["maskM"]], writes=[pstrip], extra=[pstrip.wr])
                p.rec("vector", lambda e, t=t: e.reduce_sum(
                    lsl[:, 4:5], pstrip[:, t * 1024:(t + 1) * 1024], axis=AX.X),
                    reads=[pstrip, lsl], writes=[lsl])
                if nbulk > 0:
                    p.rec("vector", lambda e: e.tensor_add(
                        lsl[:, 0:1], lsl[:, 0:1], lsl[:, 4:5]),
                        reads=[lsl], writes=[lsl])
                else:
                    p.rec("vector", lambda e: e.tensor_copy(
                        lsl[:, 0:1], lsl[:, 4:5]),
                        reads=[lsl], writes=[lsl])
                p.rec("vector", lambda e: e.reciprocal(lsl[:, 3:4], lsl[:, 0:1]),
                      reads=[lsl], writes=[lsl])
                if "dbg_ps" in io and h == 0 and t == 0:
                    p.dma("scalar", io["dbg_ps"],
                          pstrip[:, 0:1024], reads=[pstrip])
                    p.dma("scalar", io["dbg_l"], lsl[:, :], reads=[lsl])
                    p.dma("scalar", io["dbg_bias"], bias_all[:, :],
                          reads=[bias_all])
                p.rec("vector", lambda e, ext=ext: e.tensor_scalar_mul(
                    pstrip[:, 0:ext], pstrip[:, 0:ext], lsl[:, 3:4]),
                    reads=[lsl], writes=[pstrip], extra=[pstrip.wr])
                po = PO[(h * NT + t) % 2]
                ngrp = nkc // 4
                for grp in range(ngrp):
                    par = grp % 2
                    for j in range(4):
                        kc = grp * 4 + j
                        p.rec("tensor", lambda e, kc=kc, j=j, par=par:
                              e.transpose(
                            PT[par][:, j * 128:(j + 1) * 128],
                            pstrip[:, kc * 128:(kc + 1) * 128],
                            C["identb"][:, :]),
                            reads=[pstrip, C["identb"]],
                            writes=[PT[par]] if j == 3 else [],
                            extra=([PT[par].wr] + PT[par].rd) if j == 0 else ())
                    p.rec("vector", lambda e, par=par: e.tensor_copy(
                        ptb[par][:, :], PT[par][:, :]),
                        reads=[PT[par]], writes=[ptb[par]])
                    for j in range(4):
                        kc = grp * 4 + j
                        p.rec("tensor", lambda e, kc=kc, j=j, par=par, po=po,
                              first=(kc == 0), last=(kc == nkc - 1): e.matmul(
                            po[:, 0:128], vbuf[:, kc * 128:(kc + 1) * 128],
                            ptb[par][:, j * 128:(j + 1) * 128],
                            start=first, stop=last),
                            reads=[vbuf, ptb[par]],
                            writes=[po] if kc == nkc - 1 else [],
                            extra=([po.wr] + po.rd) if kc == 0 else ())
                p.rec("scalar", lambda e, h=h, t=t, po=po: e.activation(
                    att[:, h * TPC + t * 128: h * TPC + (t + 1) * 128],
                    po[:, 0:128], AF.Copy),
                    reads=[po], writes=[att], extra=[att.wr])


def emit_combine(p, al, C, PS, SC, cfg, io, att, xout_buf):
    wbuf = SC["w"]
    xa = SC["f"][0:2]
    t1 = SC["f"][2]
    ff32 = SC["f"][3]
    ffb = SC["b"][0]
    pm = PS["PM"][2:4]
    has_bias = "cbias" in io
    if has_bias:
        cb2 = al.sb("cm_cbias", [128, 2 * NDC], F32)
        p.dma("sync", cb2[:, :], io["cbias"], writes=[cb2])
    for oc in range(NDC):
        b = oc % 2
        p.dma("gpsimd", wbuf[b][:, :],
              io["WoT"][:, oc * (NDC * 128):(oc + 1) * (NDC * 128)],
              writes=[wbuf[b]])
        p.dma("sync", xa[b][:, :], io["x"][oc * 128:(oc + 1) * 128, :],
              writes=[xa[b]], reads=[io["x_dep"]] if "x_dep" in io else ())
        p.dma("sync", ffb[:, :], io["ffn"][oc * 128:(oc + 1) * 128, :],
              writes=[ffb])
        p.rec("vector", lambda e: e.tensor_copy(ff32[:, :], ffb[:, :]),
              reads=[ffb], writes=[ff32])
        for tb in range(2):
            for c in range(NDC):
                p.rec("tensor", lambda e, b=b, c=c, tb=tb: e.matmul(
                    pm[tb][:, :], wbuf[b][:, c * 128:(c + 1) * 128],
                    att[:, c * TPC + tb * 512: c * TPC + tb * 512 + 512],
                    start=(c == 0), stop=(c == NDC - 1)),
                    reads=[wbuf[b], att], writes=[pm[tb]] if c == NDC - 1 else [],
                    extra=([pm[tb].wr] + pm[tb].rd) if c == 0 else ())
        for tb in range(2):
            sl = slice(tb * 512, (tb + 1) * 512)
            if has_bias:
                p.rec("vector", lambda e, tb=tb, sl=sl, oc=oc: e.tensor_scalar(
                    t1[:, sl], pm[tb][:, :], cb2[:, oc:oc + 1],
                    cb2[:, NDC + oc:NDC + oc + 1], ALU.add, ALU.add),
                    reads=[pm[tb], cb2], writes=[t1] if tb == 1 else [],
                    extra=(t1.rd + [t1.wr]) if tb == 0 else ())
            else:
                p.rec("vector", lambda e, tb=tb, sl=sl: e.tensor_copy(
                    t1[:, sl], pm[tb][:, :]),
                    reads=[pm[tb]], writes=[t1] if tb == 1 else [],
                    extra=(t1.rd + [t1.wr]) if tb == 0 else ())
        p.rec("vector", lambda e, b=b: e.tensor_add(t1[:, :], t1[:, :],
                                                    xa[b][:, :]),
              reads=[t1, xa[b]], writes=[t1])
        p.rec("vector", lambda e: e.tensor_add(t1[:, :], t1[:, :], ff32[:, :]),
              reads=[t1, ff32], writes=[t1])
        p.dma("scalar", xout_buf.ap[oc * 128:(oc + 1) * 128, :], t1[:, :],
              reads=[t1], writes=[xout_buf])


def emit_epilogue(p, al, C, PS, SC, cfg, io, att, xout_buf):
    emit_attention(p, al, C, PS, SC, cfg, io, att)
    if "dbg_att" in io:
        p.dma("scalar", io["dbg_att"], att[:, :], reads=[att])
    emit_combine(p, al, C, PS, SC, cfg, io, att, xout_buf)



def _epilogue_io(nc, cfg, with_cbias):
    KV = cfg["kv"]
    io = {
        "qT": _dram(nc, "qT", (128, NH * TPC), BF16, "ExternalInput"),
        "kall": _dram(nc, "kall", (128, KV * SEQ), BF16, "ExternalInput"),
        "qnorm": _dram(nc, "qnorm", (NH, TPC), F32, "ExternalInput"),
        "knorm_all": _dram(nc, "knorm_all", (KV, SEQ), F32, "ExternalInput"),
        "ffn": _dram(nc, "ffn", (D, TPC), BF16, "ExternalInput"),
        "WoT": _dram(nc, "WoT", (128, NDC * NDC * 128), F32, "ExternalInput"),
    }
    vall = _dram(nc, "vall", (SEQ, KV * 128), BF16, "ExternalInput")
    io["vall_r"] = vall.rearrange("(c p) w -> p c w", p=128)
    if with_cbias:
        io["cbias"] = _dram(nc, "cbias", (128, 2 * NDC), F32, "ExternalInput")
    return io


def build_p2():
    nc = bass.Bass("TRN2", target_bir_lowering=False, debug=False)
    KV1 = L1["kv"]
    io_e = _epilogue_io(nc, L0, with_cbias=False)
    io_e["x"] = _dram(nc, "x", (D, TPC), F32, "ExternalInput")
    x1 = _dram(nc, "x1", (D, TPC), F32, "ExternalOutput")
    io_p = {
        "x": x1,
        "WqT": _dram(nc, "WqT1", (128, NH * NDC * 128), F32, "ExternalInput"),
        "WkT": _dram(nc, "WkT1", (128, KV1 * NDC * 128), F32, "ExternalInput"),
        "WvT": _dram(nc, "WvT1", (128, KV1 * NDC * 128), F32, "ExternalInput"),
        "qkvb": _dram(nc, "qkvb1", (128, 3 * NDC), F32, "ExternalInput"),
        "gateT": _dram(nc, "gateT", (128, NDC * 8), F32, "ExternalInput"),
        "ics": _dram(nc, "ics", (8, D), F32, "ExternalInput"),
        "ocs": _dram(nc, "ocs", (8, D), F32, "ExternalInput"),
        "WsT": _dram(nc, "WsT", (128, NDC * NDC * 128), F32, "ExternalInput"),
        "kT_out": _dram(nc, "kT_out1", (128, KV1 * TPC), BF16, "ExternalOutput"),
        "v_out": _dram(nc, "v_out1", (TPC, KV1 * 128), BF16, "ExternalOutput"),
        "qT_out": _dram(nc, "qT_out1", (128, NH * TPC), BF16, "ExternalOutput"),
        "knorm_out": _dram(nc, "knorm_out1", (KV1, TPC), F32, "ExternalOutput"),
        "qnorm_out": _dram(nc, "qnorm_out1", (NH, TPC), F32, "ExternalOutput"),
        "ffn_out": _dram(nc, "ffn_out1", (D, TPC), BF16, "ExternalOutput"),
    }
    for nm, arr in CONSTS.items():
        io_e[nm] = _dram(nc, "c_" + nm, arr.shape,
                         BF16 if arr.dtype == bf16 else F32, "ExternalInput")
    io_e["maskM"] = _dram(nc, "c_maskM", (128, 1024), BF16, "ExternalInput")
    for k in ("ones_col", "epscol", "ones_row", "negsc_row", "identf",
              "identb", "swapf", "expander", "maskM"):
        io_p[k] = io_e[k]

    p = P(nc)
    al = Alloc(nc)
    PS = make_pool(al)
    SC = make_scratch(al)
    C = emit_consts(p, al, io_e)
    big = al.sb("big", [128, NDC * TPC], BF16)
    xout_buf = Buf(x1, "x1")
    emit_epilogue(p, al, C, PS, SC, L0, io_e, big, xout_buf)
    emit_prologue(p, al, C, PS, SC, L1, io_p, xn=big, x_dep=xout_buf)
    p.barrier_all()
    p.finish()
    al.close()
    return nc


def build_p3():
    nc = bass.Bass("TRN2", target_bir_lowering=False, debug=False)
    io_e = _epilogue_io(nc, L1, with_cbias=True)
    io_e["x"] = _dram(nc, "x", (D, TPC), F32, "ExternalInput")
    xo = _dram(nc, "xo", (D, TPC), F32, "ExternalOutput")
    for nm, arr in CONSTS.items():
        io_e[nm] = _dram(nc, "c_" + nm, arr.shape,
                         BF16 if arr.dtype == bf16 else F32, "ExternalInput")
    io_e["maskM"] = _dram(nc, "c_maskM", (128, 1024), BF16, "ExternalInput")
    p = P(nc)
    al = Alloc(nc)
    PS = make_pool(al)
    SC = make_scratch(al)
    C = emit_consts(p, al, io_e)
    big = al.sb("big", [128, NDC * TPC], BF16)
    xout_buf = Buf(xo, "xo")
    emit_epilogue(p, al, C, PS, SC, L1, io_e, big, xout_buf)
    p.barrier_all()
    p.finish()
    al.close()
    return nc



def _consts_map(core):
    d = {("c_" + nm): arr for nm, arr in CONSTS.items()}
    d["c_maskM"] = frontier_mask(core)
    return d


def gather_k(kouts, KV):
    a = np.stack([np.asarray(k) for k in kouts])
    a = a.reshape(NCORES, 128, KV, NT, 128)
    a = a.transpose(1, 2, 3, 0, 4)
    return np.ascontiguousarray(a.reshape(128, KV * SEQ))


def gather_v(vouts, KV):
    a = np.stack([np.asarray(v) for v in vouts])
    a = a.reshape(NCORES, NT, 128, KV * 128).transpose(1, 0, 2, 3)
    return np.ascontiguousarray(a.reshape(SEQ, KV * 128))


def gather_n(nouts, KV):
    a = np.stack([np.asarray(n) for n in nouts])
    a = a.reshape(NCORES, KV, NT, 128).transpose(1, 2, 0, 3)
    return np.ascontiguousarray(a.reshape(KV, SEQ))


def p2_inputs(inputs, core, r1):
    d = {
        "x": fm_shard(np.asarray(inputs["L0_attn_hidden_states"]).reshape(SEQ, D),
                      core),
        "qT": np.asarray(r1[core]["qT_out"]),
        "kall": r1["kall"],
        "vall": r1["vall"],
        "qnorm": np.asarray(r1[core]["qnorm_out"]),
        "knorm_all": r1["knorm_all"],
        "ffn": np.asarray(r1[core]["ffn_out"]),
        "WoT": wlay(np.ascontiguousarray(np.asarray(inputs["L0_attn_Wo"]).T)),
        "WqT1": wlay(np.ascontiguousarray(np.asarray(inputs["L1_attn_Wq"]).T)),
        "WkT1": wlay(np.ascontiguousarray(np.asarray(inputs["L1_attn_Wk"]).T)),
        "WvT1": wlay(np.ascontiguousarray(np.asarray(inputs["L1_attn_Wv"]).T)),
        "qkvb1": np.ascontiguousarray(np.stack(
            [np.asarray(inputs["L1_attn_bq"]).reshape(NDC, 128).T,
             np.asarray(inputs["L1_attn_bk"]).reshape(NDC, 128).T,
             np.asarray(inputs["L1_attn_bv"]).reshape(NDC, 128).T],
            axis=1).reshape(128, 3 * NDC, order="F")) if False else
            np.ascontiguousarray(np.concatenate(
                [np.asarray(inputs["L1_attn_bq"]).reshape(NDC, 128).T,
                 np.asarray(inputs["L1_attn_bk"]).reshape(NDC, 128).T,
                 np.asarray(inputs["L1_attn_bv"]).reshape(NDC, 128).T],
                axis=1)),
        "gateT": wlay(np.ascontiguousarray(np.asarray(inputs["L1_ffn_gate_w"]).T),
                      o_tile=8),
        "ics": np.ascontiguousarray(np.asarray(inputs["L1_ffn_in_channel_scale"])),
        "ocs": np.ascontiguousarray(np.asarray(inputs["L1_ffn_out_channel_scale"])),
        "WsT": wlay(np.ascontiguousarray(np.asarray(inputs["L1_ffn_weight"]).T)),
    }
    d.update(_consts_map(core))
    return d


def p3_inputs(inputs, core, r2):
    bo = np.asarray(inputs["L1_attn_bo"]).reshape(NDC, 128).T
    fb = np.asarray(inputs["L1_ffn_bias"]).reshape(NDC, 128).T
    d = {
        "x": np.asarray(r2[core]["x1"]),
        "qT": np.asarray(r2[core]["qT_out1"]),
        "kall": r2["kall"],
        "vall": r2["vall"],
        "qnorm": np.asarray(r2[core]["qnorm_out1"]),
        "knorm_all": r2["knorm_all"],
        "ffn": np.asarray(r2[core]["ffn_out1"]),
        "WoT": wlay(np.ascontiguousarray(np.asarray(inputs["L1_attn_Wo"]).T)),
        "cbias": np.ascontiguousarray(np.concatenate([bo, fb], axis=1)),
    }
    d.update(_consts_map(core))
    return d


def kernel(**inputs):
    nc1 = build_p1()
    maps1 = [p1_inputs(inputs, c) for c in range(NCORES)]
    r1 = run_bass_kernel_spmd(nc1, maps1, core_ids=list(range(NCORES))).results
    g1 = {c: r1[c] for c in range(NCORES)}
    g1["kall"] = gather_k([r1[c]["kT_out"] for c in range(NCORES)], 8)
    g1["vall"] = gather_v([r1[c]["v_out"] for c in range(NCORES)], 8)
    g1["knorm_all"] = gather_n([r1[c]["knorm_out"] for c in range(NCORES)], 8)

    nc2 = build_p2()
    maps2 = [p2_inputs(inputs, c, g1) for c in range(NCORES)]
    r2 = run_bass_kernel_spmd(nc2, maps2, core_ids=list(range(NCORES))).results
    g2 = {c: r2[c] for c in range(NCORES)}
    g2["kall"] = gather_k([r2[c]["kT_out1"] for c in range(NCORES)], 32)
    g2["vall"] = gather_v([r2[c]["v_out1"] for c in range(NCORES)], 32)
    g2["knorm_all"] = gather_n([r2[c]["knorm_out1"] for c in range(NCORES)], 32)

    nc3 = build_p3()
    maps3 = [p3_inputs(inputs, c, g2) for c in range(NCORES)]
    r3 = run_bass_kernel_spmd(nc3, maps3, core_ids=list(range(NCORES))).results
    out = unshard_fm([np.asarray(r3[c]["xo"], np.float32)
                      for c in range(NCORES)])
    return out.reshape(1, SEQ, D)
